# version 03 (final)
# baseline (speedup 1.0000x reference)
import contextlib
import numpy as np
import ml_dtypes
import concourse.bass as bass
import concourse.mybir as mybir
from concourse.bass_utils import run_bass_kernel_spmd

F32 = mybir.dt.float32
BF16 = mybir.dt.bfloat16
AF = mybir.ActivationFunctionType
ALU = mybir.AluOpType
AX = mybir.AxisListType
BF = ml_dtypes.bfloat16

T = 4096
D = 1024
NCH = 8
TT = 512
NTILE = T // TT
HH = 256
NO = 32
NH = 16
EPS = 1e-5

ENGINES = ("tensor", "vector", "scalar", "sync", "gpsimd")

import os
RUN_ATTN_A = True
RUN_FFN1 = os.environ.get("K_FFN1", "1") == "1"
RUN_ATTN_B = os.environ.get("K_ATTNB", "1") == "1"
RUN_FFN2 = os.environ.get("K_FFN2", "1") == "1"
STOP_AT = os.environ.get("K_STOP", "full")
N_CORES = int(os.environ.get("K_CORES", "8"))
K_TILES = int(os.environ.get("K_TILES", str(T // TT)))
DEBUG_TAPS = os.environ.get("K_TAPS", "0") == "1"
K_SERIAL = os.environ.get("K_SERIAL", "1") == "1"
STAGES = ["ln", "proj", "scores", "softmax", "etr1", "etr2", "etr3", "etr", "vtr", "mix", "full"]
def stage_ge(st):
    return STAGES.index(STOP_AT) >= STAGES.index(st)


class Sched:

    def __init__(self):
        self.q = {e: [] for e in ENGINES}
        self.c = {}

    def inc(self, sem, n=1):
        self.c[sem] = self.c.get(sem, 0) + n
        return self.c[sem]

    def val(self, sem):
        return self.c.get(sem, 0)

    def em(self, eng, fn, waits=(), inc=None):
        self.q[eng].append((fn, tuple(w for w in waits if w is not None), inc))

    def step(self, eng, fn, waits=(), sem=None, n=1):
        if K_SERIAL:
            waits = list(waits) + [(s_, v_) for s_, v_ in self.c.items()]
        if sem is None:
            self.em(eng, fn, waits)
            return None
        v = self.inc(sem, n)
        self.em(eng, fn, waits, (sem, n))
        return v

    def dma(self, queue_eng, qsem, out, in_, waits=()):
        def fn(e, out=out, in_=in_):
            return e.dma_start(out=out, in_=in_)
        return self.step(queue_eng, fn, list(waits), sem=qsem, n=16)


def build_program():
    nc = bass.Bass("TRN2", target_bir_lowering=False, debug=False)
    S = Sched()

    def din(name, shape, dtype=F32):
        return nc.dram_tensor(name, list(shape), dtype, kind="ExternalInput").ap()

    x_in = din("x_t", [D, T])
    out_t = nc.dram_tensor("out_t", [D, T], F32, kind="ExternalOutput").ap()

    wq_d = din("wq_h", [128, 8, 8, 128])
    wdn_d = din("wdn_h", [128, 2, 8, 128])
    wup_d = din("wup_h", [128, 16, 2, 128])
    wout_d = din("wout_h", [128, 8, 8, 128])
    bcol_d = din("bcol_hbm", [128, 64])
    brow_d = din("brow_hbm", [1, 2048])
    onesrow_d = din("onesrow", [1, TT])
    identity_d = din("identity", [128, 128])
    identz_d = din("identz_h", [128, 128])

    sw1_d = din("sw1_h", [128, 16, 8, 128])
    sw3_d = din("sw3_h", [128, 16, 8, 128])
    sw2_d = din("sw2_h", [128, 8, 16, 128])
    w1_d = din("w1_h", [8, 128, 8, 8, 128])
    w2_d = din("w2_h", [8, 128, 8, 8, 128])
    bffn_d = din("bffn_hbm", [1, 8192])
    bexp_d = din("bexp_hbm", [1, 16384])
    bexpc_d = din("bexpc_hbm", [128, 64])

    win_d = din("win_h", [128, 16, 8, 128])
    wob_d = din("wob_h", [128, 8, 8, 128])
    xpw1_d = din("xpw1_h", [128, 8, 128])
    xpw2_d = din("xpw2_h", [128, 8, 128])
    dtw_d = din("dtw_h", [64, 8, 128])
    pp_d = din("pp_hbm", [128, 64])

    q_dram = nc.dram_tensor("q_scr", [NH, 64, T], BF16, kind="Internal").ap()
    e_dram = nc.dram_tensor("e_scr", [T, 16, 16], BF16, kind="Internal").ap()
    a_dram = nc.dram_tensor("a_scr", [NH, 64, T], BF16, kind="Internal").ap()
    delta_dram = nc.dram_tensor("d_scr", [128, 8, 8, 512], BF16, kind="Internal").ap()

    sb = nc.alloc_sbuf_tensor
    x_res = sb("x_res", [128, NCH * T], BF16)
    ident = sb("ident", [128, 128], BF16)
    identz = sb("identz", [128, 128], BF16)
    ones_col = sb("ones_col", [128, 128], BF16)
    ones_row = sb("ones_row", [1, TT], BF16)
    bias_row = sb("bias_row", [1, 2048], BF16)
    b2row = sb("b2row", [1, 1024], BF16)
    bcols = sb("bcols", [128, 64], F32)
    b1col = sb("b1col", [128, 8], F32)
    perpart = sb("perpart", [128, 64], F32)

    cst = sb("cst", [128, 8], F32)
    zcol = sb("zcol", [128, 1], F32)
    mu_b = sb("mu_b", [128, TT], F32)
    var_f = sb("var_f", [128, TT], F32)
    var2 = sb("var2", [128, TT], F32)
    smx = sb("smx", [128, 80], F32)
    xsq = sb("xsq", [128, 2 * TT], BF16)
    r8 = sb("r8", [128, NTILE * TT], BF16)
    mur8 = sb("mur8", [128, NTILE * TT], BF16)
    xhat = sb("xhat", [128, NCH * TT], BF16)
    afm = xhat

    PHS_ELS = 52736
    phs = sb("phs", [128, PHS_ELS], BF16)

    class PA:
        def __init__(self):
            self.off = 0
        def take(self, els, rows=128, row0=0):
            o = self.off
            self.off += els
            assert self.off <= PHS_ELS, f"phase scratch overflow {self.off}"
            return phs[row0:row0 + rows, o:o + els]

    pa_a = PA()
    wq_g = pa_a.take(2 * 8 * 128)
    wout_g = pa_a.take(2 * 8 * 128)
    wdn = pa_a.take(2 * 8 * 128)
    wup = pa_a.take(16 * 2 * 128)
    q_tmp = pa_a.take(8 * TT)
    _kv = pa_a.off
    kbuf = pa_a.take(16 * TT, rows=64)
    v_buf = phs[64:128, _kv:_kv + 16 * TT]
    lat_bf = pa_a.take(2 * TT)
    _qa = pa_a.off
    q_slab = pa_a.take(NH * TT, rows=64)
    a_slab = phs[64:128, _qa:_qa + NH * TT]
    e_cmp = pa_a.take(64 * 16)
    e_gm = pa_a.take(HH * 16, rows=16)
    e_bd = pa_a.take(NO * 128)
    v_gm = pa_a.take(NO * 128)

    pb = [nc.alloc_psum_tensor(f"pb{i}", [128, 512], F32) for i in range(6)]
    pt = [nc.alloc_psum_tensor(f"pt{i}", [128, 1024], BF16) for i in range(2)]

    qv = q_slab.rearrange("p (h t) -> p t h", h=NH)
    kvv = kbuf.rearrange("p (g t) -> p t g", g=16)

    S.dma("sync", "sq", perpart[:], pp_d)
    S.dma("sync", "sq", bcols[:], bcol_d)
    S.dma("gpsimd", "sg", x_res[:].rearrange("p (c t) -> p c t", c=NCH),
          x_in.rearrange("(c p) t -> p c t", p=128))
    v_xload = S.val("sg")
    S.dma("gpsimd", "sg", ident[:], identity_d)
    S.dma("gpsimd", "sg", identz[:], identz_d)
    S.dma("gpsimd", "sg", bias_row[:], brow_d)
    S.step("vector", lambda e: e.memset(ones_row[:], 1.0), sem="VE")
    v_ones = S.step("vector", lambda e: e.memset(ones_col[:], 1.0), sem="VE")
    for dst, src in ((wdn, wdn_d), (wup, wup_d)):
        S.dma("gpsimd", "sg", dst, src.rearrange("p a k m -> p (a k m)"))
    v_wload_a = S.val("sg")
    S.step("vector", lambda e: e.memset(cst[:, 0:1], EPS), sem="VE")
    S.step("vector", lambda e: e.memset(cst[:, 1:2], -0.5), sem="VE")
    S.step("vector", lambda e: e.memset(cst[:, 2:3], 0.125), sem="VE")
    S.step("vector", lambda e: e.memset(cst[:, 3:4], 1.0 / D), sem="VE")
    S.step("vector", lambda e: e.memset(cst[:, 4:5], 0.5), sem="VE")
    S.step("vector", lambda e: e.memset(cst[:, 5:6], 0.25), sem="VE")
    S.step("vector", lambda e: e.memset(cst[:, 6:7], 1.0), sem="VE")
    v_cst = S.step("vector", lambda e: e.memset(cst[:, 7:8], -1.0), sem="VE")
    v_zc = S.step("vector", lambda e: e.memset(zcol[:], 0.0), sem="VE")
    v_zer1 = S.step("vector", lambda e: e.memset(e_bd[:], 0.0), sem="VE")
    v_zer2 = S.step("vector", lambda e: e.memset(v_gm[:], 0.0), sem="VE")

    def ln_prepass(src_fn, first_waits=()):
        ready = {}
        prev_rowmath_v = None
        sq_mm_done = {}
        for i in range(NTILE):
            vt1 = None
            for c in range(NCH):
                w = []
                if c == 0:
                    if i == 0:
                        w += list(first_waits)
                    if prev_rowmath_v is not None:
                        w.append(("VE", prev_rowmath_v))
                vt1 = S.step("tensor", lambda e, i=i, c=c: e.matmul(
                    pb[0][:, 0:TT], ones_col[:], src_fn(i, c),
                    start=(c == 0), stop=(c == NCH - 1)), waits=w, sem="TE")
            vt2 = None
            va = None
            for c in range(NCH):
                w = []
                if i == 0 and c == 0:
                    w += list(first_waits)
                if sq_mm_done.get(c % 2) is not None:
                    w.append(("TE", sq_mm_done[c % 2]))
                va = S.step("scalar", lambda e, i=i, c=c: e.activation(
                    out=xsq[:, (c % 2) * TT:(c % 2 + 1) * TT], in_=src_fn(i, c),
                    func=AF.Square), waits=w, sem="AE")
                vt2 = S.step("tensor", lambda e, c=c: e.matmul(
                    pb[1][:, 0:TT], ones_col[:],
                    xsq[:, (c % 2) * TT:(c % 2 + 1) * TT],
                    start=(c == 0), stop=(c == NCH - 1)),
                    waits=[("AE", va)], sem="TE")
                sq_mm_done[c % 2] = vt2
            S.step("vector", lambda e: e.tensor_scalar(
                out=mu_b[:, 0:TT], in0=pb[0][:, 0:TT], scalar1=cst[:, 3:4],
                scalar2=None, op0=ALU.mult),
                waits=[("TE", vt2), ("VE", v_cst)], sem="VE")
            S.step("vector", lambda e: e.tensor_scalar(
                out=var_f[:, 0:TT], in0=pb[1][:, 0:TT], scalar1=cst[:, 3:4],
                scalar2=None, op0=ALU.mult), sem="VE")
            S.step("vector", lambda e: e.tensor_mul(
                out=var2[:, 0:TT], in0=mu_b[:, 0:TT], in1=mu_b[:, 0:TT]),
                sem="VE")
            vv = S.step("vector", lambda e: e.tensor_sub(
                out=var_f[:, 0:TT], in0=var_f[:, 0:TT], in1=var2[:, 0:TT]),
                sem="VE")
            S.step("scalar", lambda e: e.activation(
                out=var_f[:, 0:TT], in_=var_f[:, 0:TT], func=AF.Ln, bias=cst[:, 0:1]),
                waits=[("VE", vv), ("VE", v_cst)], sem="AE")
            var8 = S.step("scalar", lambda e, i=i: e.activation(
                out=r8[:, i * TT:(i + 1) * TT], in_=var_f[:, 0:TT],
                func=AF.Exp, scale=cst[:, 1:2]), sem="AE")
            vmr = S.step("vector", lambda e, i=i: e.tensor_mul(
                out=mur8[:, i * TT:(i + 1) * TT], in0=mu_b[:, 0:TT],
                in1=r8[:, i * TT:(i + 1) * TT]), waits=[("AE", var8)], sem="VE")
            prev_rowmath_v = vmr
            ready[i] = vmr
        return ready

    def ln_apply(i, src_fn, ln_ready, extra_waits=()):
        v = None
        for c in range(NCH):
            w = list(extra_waits) + [("VE", ln_ready[i])] if c == 0 else []
            S.step("vector", lambda e, i=i, c=c: e.tensor_mul(
                out=xhat[:, c * TT:(c + 1) * TT], in0=src_fn(i, c),
                in1=r8[:, i * TT:(i + 1) * TT]), waits=w, sem="VE")
            v = S.step("vector", lambda e, i=i, c=c: e.tensor_sub(
                out=xhat[:, c * TT:(c + 1) * TT],
                in0=xhat[:, c * TT:(c + 1) * TT],
                in1=mur8[:, i * TT:(i + 1) * TT]), sem="VE")
        return v

    def proj(w_fn, n_oc, n_k, rhs_fn, bank_sel, out_cb,
             first_waits=(), n_tok=TT, krow=None, oc_waits=None):
        bank_free = {}
        last = None
        for oc in range(n_oc):
            bk = bank_sel[oc % len(bank_sel)]
            w = []
            if oc < len(bank_sel):
                w += list(first_waits)
            if oc_waits:
                ocw = oc_waits(oc) if callable(oc_waits) else oc_waits.get(oc)
                if ocw:
                    w += list(ocw)
            if bank_free.get(bk) is not None:
                w.append(bank_free[bk])
            for k in range(n_k):
                S.step("tensor", lambda e, oc=oc, k=k, bk=bk: e.matmul(
                    pb[bk][:, 0:n_tok], w_fn(oc, k), rhs_fn(k),
                    start=(k == 0), stop=(krow is None and k == n_k - 1)),
                    waits=w if k == 0 else (), sem="TE")
            if krow is not None:
                vt = S.step("tensor", lambda e, oc=oc, bk=bk: e.matmul(
                    pb[bk][:, 0:n_tok], krow(oc), ones_row[0:1, 0:n_tok],
                    start=False, stop=True), sem="TE")
            else:
                vt = S.val("TE")
            bank_free[bk] = out_cb(oc, bk, vt)
            last = bank_free[bk]
        return last

    la = {}

    fence_buf = cst

    def ae_fence(waits=()):
        return S.step("scalar", lambda e: e.activation(
            out=smx[:, 72:73], in_=smx[:, 72:73], func=AF.Copy),
            waits=waits, sem="AE")

    def te_fence(waits=()):
        return S.step("tensor", lambda e: e.matmul(
            pb[0][0:1, 508:509], ones_col[0:1, 0:1], ones_col[0:1, 0:1],
            start=True, stop=True), waits=waits, sem="TE")

    def phase_attn_a():
        ln_ready = ln_prepass(
            lambda i, c: x_res[:, c * T + i * TT: c * T + i * TT + TT],
            first_waits=[("sg", v_xload), ("sq", S.val("sq")), ("sg", v_wload_a), ("VE", v_ones)])

        for i in range(K_TILES):
            t0 = i * TT
            vap = ln_apply(
                i, lambda i, c: x_res[:, c * T + i * TT: c * T + i * TT + TT],
                ln_ready, extra_waits=[la.get("xhat_consumed")])

            if not stage_ge("proj"):
                la["last"] = ("VE", ln_ready[i])
                continue
            vg_q = {}

            def emit_wq_gran(oc, w):
                vg_q[oc] = S.dma("gpsimd", "sg",
                                 wq_g[:, (oc % 2) * 1024:(oc % 2 + 1) * 1024]
                                 .rearrange("p (k m) -> p k m", k=8),
                                 wq_d[:, oc],
                                 waits=w)
            for oc in range(2):
                w = [la[f"wqmm{oc % 2}"]] if la.get(f"wqmm{oc % 2}") else []
                emit_wq_gran(oc, w)

            def q_out(oc, bk, vt):
                la[f"wqmm{oc % 2}"] = ("TE", vt)
                if oc + 2 < 8:
                    emit_wq_gran(oc + 2, [("TE", vt)])
                w = [("TE", vt)]
                if oc == 0 and la.get("qup") is not None:
                    w.append(la["qup"])
                va = S.step("scalar", lambda e, oc=oc, bk=bk: e.activation(
                    out=q_tmp[:, oc * TT:(oc + 1) * TT], in_=pb[bk][:],
                    func=AF.Identity, bias=bcols[:, oc:oc + 1]),
                    waits=w, sem="AE")
                return ("AE", va)
            proj(lambda oc, k: wq_g[:, (oc % 2) * 1024 + k * 128:
                                    (oc % 2) * 1024 + (k + 1) * 128],
                 8, 8, lambda k: xhat[:, k * TT:(k + 1) * TT],
                 (2, 3), q_out,
                 first_waits=[("VE", vap), la.get("pb23_free")],
                 oc_waits=lambda oc: [("sg", vg_q[oc])])
            v_qevac = S.val("AE")
            for hp in range(2):
                vu = S.dma("sync", "sq",
                           q_dram[hp * 8:(hp + 1) * 8, :, t0:t0 + TT]
                           .rearrange("c d t -> d c t"),
                           q_tmp[hp * 64:(hp + 1) * 64, :]
                           .rearrange("d (c t) -> d c t", c=8),
                           waits=[("AE", v_qevac)])
            la["qup"] = ("sq", vu)
            v_qdown = S.dma("sync", "sq",
                            q_slab.rearrange("d (h t) -> d h t", h=NH),
                            q_dram[:, :, t0:t0 + TT].rearrange("h d t -> d h t"),
                            waits=[la.get("scores_done"), la["qup"]])

            def lat_out(oc, bk, vt):
                va = S.step("scalar", lambda e, oc=oc, bk=bk: e.activation(
                    out=lat_bf[:, oc * TT:(oc + 1) * TT], in_=pb[bk][:],
                    func=AF.Identity, bias=bcols[:, 8 + oc:9 + oc]),
                    waits=[("TE", vt)], sem="AE")
                return ("AE", va)
            proj(lambda oc, k: wdn[:, (oc * 8 + k) * 128:(oc * 8 + k + 1) * 128],
                 2, 8, lambda k: xhat[:, k * TT:(k + 1) * TT], (2, 3), lat_out)
            v_latevac = S.val("AE")

            def kv_out(oc, bk, vt):
                S.step("scalar", lambda e, oc=oc, bk=bk: e.activation(
                    out=kbuf[:, oc * TT:(oc + 1) * TT], in_=pb[bk][0:64, :],
                    func=AF.Identity, bias=bcols[0:64, 10 + oc:11 + oc]),
                    waits=[("TE", vt)], sem="AE")
                va = S.step("scalar", lambda e, oc=oc, bk=bk: e.activation(
                    out=v_buf.rearrange("p (t g) -> p t g", g=16)[:, :, oc],
                    in_=pb[bk][64:128, :],
                    func=AF.Identity, bias=bcols[64:128, 10 + oc:11 + oc]),
                    sem="AE")
                return ("AE", va)
            proj(lambda oc, k: wup[:, (oc * 2 + k) * 128:(oc * 2 + k + 1) * 128],
                 16, 2, lambda k: lat_bf[:, k * TT:(k + 1) * TT],
                 (2, 3), kv_out, first_waits=[("AE", v_latevac)])
            v_kvevac = S.val("AE")

            la["last"] = ("AE", v_kvevac)
            if not stage_ge("scores"):
                continue
            a_last_evac = None
            for hf in range(2):
                th0 = hf * HH
                w0 = [("sq", v_qdown), ("AE", v_kvevac)]
                if la.get("softmax_done") is not None:
                    w0.append(la["softmax_done"])
                for pk_ in (4, 5):
                    if la.get(f"pq{pk_}_evac") is not None:
                        w0.append(la[f"pq{pk_}_evac"])
                vsc = None
                for tl in range(HH):
                    t_loc = th0 + tl
                    tq = t_loc % 4
                    qh = tl // 4
                    bk = 4 + qh // 32
                    col = (qh % 32) * 16
                    vsc = S.step("tensor",
                                 lambda e, t_loc=t_loc, tq=tq, bk=bk, col=col:
                                 e.matmul(
                                     pb[bk][32 * tq:32 * tq + 16, col:col + 16],
                                     qv[:, t_loc, :], kvv[0:64, t_loc, :],
                                     start=True, stop=True,
                                     tile_position=(0, 32 * tq)),
                                 waits=w0 if tl == 0 else (), sem="TE")
                la["last"] = ("TE", vsc)
                if not stage_ge("softmax"):
                    continue
                vsm_all = None
                for b in range(2):
                    bk = 4 + b
                    ec0 = b * 512
                    vev = S.step("scalar", lambda e, bk=bk, b=b: e.activation(
                        out=(var_f if b == 0 else var2)[:, 0:512],
                        in_=pb[bk][:], func=AF.Copy),
                        waits=[("TE", vsc)], sem="AE")
                    S.step("vector", lambda e, b=b: e.reduce_max(
                        out=smx[:, b * 8:b * 8 + 1],
                        in_=(var_f if b == 0 else var2)[:, 0:512],
                        axis=AX.X), waits=[("AE", vev)], sem="VE")
                    vm2 = S.step("vector", lambda e, b=b: e.tensor_scalar(
                        out=(var_f if b == 0 else var2)[:, 0:512],
                        in0=(var_f if b == 0 else var2)[:, 0:512],
                        scalar1=smx[:, b * 8:b * 8 + 1], scalar2=-600.0,
                        op0=ALU.subtract, op1=ALU.max), sem="VE")
                    wv1 = [("VE", vm2)]
                    if b == 0 and la.get("ecmp_consumed") is not None:
                        wv1.append(la["ecmp_consumed"])
                    va1 = S.step("scalar", lambda e, b=b, ec0=ec0: e.activation(
                        out=e_cmp[:, ec0:ec0 + 512],
                        in_=(var_f if b == 0 else var2)[:, 0:512],
                        func=AF.Exp, scale=cst[:, 2:3]), waits=wv1, sem="AE")
                    S.step("vector", lambda e, ec0=ec0, b=b: e.reduce_sum(
                        out=smx[:, 16 + b * 32:16 + b * 32 + 32],
                        in_=e_cmp[:, ec0:ec0 + 512].rearrange(
                            "p (q g) -> p q g", g=16),
                        axis=AX.X), waits=[("AE", va1)], sem="VE")
                    S.step("vector", lambda e, b=b: e.reciprocal(
                        out=smx[:, 16 + b * 32:16 + b * 32 + 32],
                        in_=smx[:, 16 + b * 32:16 + b * 32 + 32]), sem="VE")
                    vsm_all = S.step("vector", lambda e, ec0=ec0, b=b: e.tensor_mul(
                        out=e_cmp[:, ec0:ec0 + 512].rearrange(
                            "p (q g) -> p q g", g=16),
                        in0=e_cmp[:, ec0:ec0 + 512].rearrange(
                            "p (q g) -> p q g", g=16),
                        in1=smx[:, 16 + b * 32:16 + b * 32 + 32]
                        .unsqueeze(2).broadcast_to([128, 32, 16])), sem="VE")
                la["softmax_done"] = ("VE", vsm_all)

                la["last"] = ("VE", vsm_all)
                if not stage_ge("etr1"):
                    continue
                v_egm = None
                for qb in range(16):
                    pk = 4 + qb % 2
                    wv = [("VE", vsm_all)] if qb < 2 else []
                    if la.get(f"pq{pk}_evac") is not None:
                        wv.append(la[f"pq{pk}_evac"])
                    vtp = None
                    for j in range(4):
                        qh = qb * 4 + j
                        vtp = S.step("tensor", lambda e, qh=qh, j=j, pk=pk: e.matmul(
                            pb[pk][0:16, j * 128:(j + 1) * 128],
                            e_cmp[:, qh * 16:(qh + 1) * 16], ident[:],
                            start=(j == 0), stop=(j == 3),
                            skip_group_check=True),
                            waits=wv if j == 0 else (), sem="TE")
                    if not stage_ge("etr2"):
                        la[f"pq{pk}_evac"] = ("TE", vtp)
                        continue
                    wv2 = [("TE", vtp)]
                    if qb < 2 and la.get("eup") is not None:
                        wv2.append(la["eup"])
                    v_egm = S.step("scalar", lambda e, qb=qb, pk=pk: e.activation(
                        out=e_gm[:, qb * 256:(qb + 1) * 256],
                        in_=pb[pk][0:16, :].rearrange(
                            "p (j tq x) -> p j tq x", j=4, tq=4)[:, :, :, 0:16],
                        func=AF.Copy), waits=wv2, sem="AE")
                    la[f"pq{pk}_evac"] = ("AE", v_egm)
                la["ecmp_consumed"] = ("TE", S.val("TE"))
                if not stage_ge("etr2"):
                    la["last"] = ("TE", S.val("TE"))
                    continue
                la["last"] = ("AE", v_egm)
                if not stage_ge("etr3"):
                    continue
                v_eup = S.dma("sync", "sq",
                              e_dram[t0 + th0: t0 + th0 + HH, :, :]
                              .rearrange("t g h -> g t h"),
                              e_gm.rearrange("g (t h) -> g t h", h=16),
                              waits=[("AE", v_egm)])
                la["eup"] = ("sq", v_eup)
                la["last"] = ("sq", v_eup)
                vdn = None
                for tau in range(8):
                    wv = []
                    if tau == 0:
                        wv.append(("sq", v_eup))
                        if la.get("mix_done") is not None:
                            wv.append(la["mix_done"])
                        wv.append(("VE", v_zer1))
                    vdn = S.dma("sync", "sq",
                                e_bd[16 * tau:16 * tau + 16, :]
                                .rearrange("g (o m) -> g o m", m=128)
                                [:, :, tau * 16: tau * 16 + 16],
                                e_dram[t0 + th0: t0 + th0 + HH, :, :]
                                .rearrange("(o tau) g h -> tau g o h", tau=8)
                                [tau],
                                waits=wv)
                v_edown = vdn
                la["last"] = ("sq", v_edown)
                if not stage_ge("vtr"):
                    continue
                v_vgm = None
                for ob in range(8):
                    pk = ob % 2
                    wv = []
                    if ob == 0:
                        wv.append(("AE", v_kvevac))
                    if la.get(f"pv{pk}_evac") is not None:
                        wv.append(la[f"pv{pk}_evac"])
                    vtp = None
                    for j in range(4):
                        o = ob * 4 + j
                        c0 = (th0 + 8 * o) * 16
                        vtp = S.step("tensor", lambda e, c0=c0, j=j, pk=pk: e.matmul(
                            pb[pk][:, j * 128:(j + 1) * 128],
                            v_buf[:, c0:c0 + 128],
                            identz[64:128, :],
                            start=(j == 0), stop=(j == 3),
                            skip_group_check=True),
                            waits=wv if j == 0 else (), sem="TE")
                    wv2 = [("TE", vtp)]
                    if la.get("mix_done") is not None:
                        wv2.append(la["mix_done"])
                    v_vgm = S.step("scalar", lambda e, ob=ob, pk=pk: e.activation(
                        out=v_gm[:, ob * 512:(ob + 1) * 512],
                        in_=pb[pk][:], func=AF.Copy),
                        waits=wv2, sem="AE")
                    la[f"pv{pk}_evac"] = ("AE", v_vgm)
                la["last"] = ("AE", v_vgm)
                if not stage_ge("mix"):
                    continue
                v_mix_last = None
                for o in range(NO):
                    bk = 2 + (o // 4) % 2
                    w = []
                    if o == 0:
                        w += [("AE", v_vgm), ("sq", v_edown)]
                    if o % 4 == 0 and la.get(f"mixev{bk}") is not None:
                        w.append(la[f"mixev{bk}"])
                    vmx = S.step("tensor", lambda e, o=o, bk=bk: e.matmul(
                        pb[bk][:, (o % 4) * 128:(o % 4 + 1) * 128],
                        v_gm[:, o * 128:(o + 1) * 128],
                        e_bd[:, o * 128:(o + 1) * 128],
                        start=True, stop=True), waits=w, sem="TE")
                    v_mix_last = vmx
                    if o % 4 == 3:
                        o0 = o - 3
                        wv = [("TE", vmx)]
                        if la.get("aup") is not None and o0 == 0 and hf == 0:
                            wv.append(la["aup"])
                        va = S.step("scalar", lambda e, o0=o0, bk=bk, hf=hf:
                                    e.activation(
                                        out=a_slab.rearrange(
                                            "d (h half o tau) -> d half o tau h",
                                            h=NH, half=2, tau=8)
                                        [:, hf, o0:o0 + 4, :, :],
                                        in_=pb[bk][0:64, :].rearrange(
                                            "d (o tau h) -> d o tau h",
                                            o=4, h=16),
                                        func=AF.Copy), waits=wv, sem="AE")
                        la[f"mixev{bk}"] = ("AE", va)
                        a_last_evac = va
                la["mix_done"] = ("TE", v_mix_last)
                la["scores_done"] = ("TE", v_mix_last)
                la["pb23_free"] = ("AE", S.val("AE"))

            if not stage_ge("full"):
                continue
            v_aup = S.dma("sync", "sq",
                          a_dram[:, :, t0:t0 + TT].rearrange("h d t -> d h t"),
                          a_slab.rearrange("d (h t) -> d h t", h=NH),
                          waits=[("AE", a_last_evac)])
            la["aup"] = ("sq", v_aup)
            v_te_kv = S.val("TE")
            vdn = None
            for c in range(NCH):
                wv = []
                if c == 0:
                    wv.append(("sq", v_aup))
                    if la.get("wout_done") is not None:
                        wv.append(la["wout_done"])
                    wv.append(("TE", v_te_kv))
                vdn = S.dma("sync", "sq",
                            afm[:, c * TT:(c + 1) * TT],
                            a_dram.rearrange("h d t -> (h d) t")
                            .rearrange("(c p) t -> p c t", p=128)[:, c, t0:t0 + TT],
                            waits=wv)
            v_adown = vdn
            vg_o = {}

            def emit_wo_gran(oc, w):
                vg_o[oc] = S.dma("gpsimd", "sg",
                                 wout_g[:, (oc % 2) * 1024:(oc % 2 + 1) * 1024]
                                 .rearrange("p (k m) -> p k m", k=8),
                                 wout_d[:, oc],
                                 waits=w)
            for oc in range(2):
                w = [la[f"womm{oc % 2}"]] if la.get(f"womm{oc % 2}") else []
                emit_wo_gran(oc, w)
            res_vals = []
            def wout_out(oc, bk, vt):
                la[f"womm{oc % 2}"] = ("TE", vt)
                if oc + 2 < 8:
                    emit_wo_gran(oc + 2, [("TE", vt)])
                vr = S.step("vector", lambda e, oc=oc, bk=bk, t0=t0:
                            e.tensor_add(
                                out=x_res[:, oc * T + t0: oc * T + t0 + TT],
                                in0=x_res[:, oc * T + t0: oc * T + t0 + TT],
                                in1=pb[bk][:]),
                            waits=[("TE", vt)], sem="VE")
                res_vals.append(vr)
                return ("VE", vr)
            proj(lambda oc, k: wout_g[:, (oc % 2) * 1024 + k * 128:
                                      (oc % 2) * 1024 + (k + 1) * 128],
                 8, 8, lambda k: afm[:, k * TT:(k + 1) * TT],
                 (2, 3), wout_out,
                 first_waits=[("sq", v_adown)],
                 krow=lambda oc: bias_row[0:1, oc * 128:(oc + 1) * 128],
                 oc_waits=lambda oc: [("sg", vg_o[oc])])
            la["wout_done"] = ("VE", res_vals[-1])
            la["pb23_free"] = ("VE", res_vals[-1])
            la["xhat_consumed"] = ("TE", S.val("TE"))
            la["last"] = la["wout_done"]
        if DEBUG_TAPS:
            taps = {
                "tp_xhat": (xhat[:, 0:8 * TT], [128, 8 * TT]),
                "tp_qtmp": (q_tmp[:, 0:8 * TT], [128, 8 * TT]),
                "tp_qslab": (q_slab, [64, NH * TT]),
                "tp_kbuf": (kbuf, [64, 16 * TT]),
                "tp_vbuf": (v_buf, [64, TT * 16]),
                "tp_ecmp": (e_cmp[:, 0:1024], [128, 1024]),
                "tp_egm": (e_gm[:, 0:4096], [16, 4096]),
                "tp_ebd": (e_bd[:, 0:4096], [128, 4096]),
                "tp_vgm": (v_gm[:, 0:4096], [128, 4096]),
                "tp_aslab": (a_slab, [64, NH * TT]),
                "tp_afm": (afm[:, 0:8 * TT], [128, 8 * TT]),
                "tp_r8": (r8[:, 0:TT], [128, TT]),
                "tp_mur8": (mur8[:, 0:TT], [128, TT]),
            }
            wv = [la["last"], ("AE", S.val("AE")), ("VE", S.val("VE")),
                  ("TE", S.val("TE"))]
            for nm, (ap, shp) in taps.items():
                dt_out = nc.dram_tensor(nm, shp, BF16, kind="ExternalOutput").ap()
                S.dma("sync", "sq", dt_out, ap, waits=wv)
                wv = []
            la["last"] = ("sq", S.val("sq"))
        return la["last"]

    def phase_ffn(call_id):
        bar = [(k, S.val(k)) for k in ("TE", "VE", "AE", "sq", "sg")]
        ln_ready = ln_prepass(
            lambda i, c: x_res[:, c * T + i * TT: c * T + i * TT + TT],
            first_waits=bar)

        pe_ = PA()
        u_g = pe_.take(2 * 8 * TT)
        h_e = pe_.take(8 * 1024)
        w1g = pe_.take(2 * 8 * 128)
        w2g = pe_.take(2 * 8 * 128)
        te3 = pe_.take(2048)
        ebar = [(k, S.val(k)) for k in ("TE", "VE", "AE", "sq", "sg")]
        vlnall = ("VE", S.val("VE"))

        def gather_xhat(r, slot, waits):
            v = None
            xr = x_res[:].rearrange("p (c o e8) -> p c o e8", c=NCH, e8=8)
            rr = r8[:].rearrange("p (o e8) -> p o e8", e8=8)
            mm = mur8[:].rearrange("p (o e8) -> p o e8", e8=8)
            for c in range(NCH):
                S.step("vector", lambda e, c=c, r=r, slot=slot: e.tensor_mul(
                    out=u_g[:, slot * 8 * TT + c * TT: slot * 8 * TT + (c + 1) * TT],
                    in0=xr[:, c, :, r], in1=rr[:, :, r]),
                    waits=waits if c == 0 else (), sem="VE")
                v = S.step("vector", lambda e, c=c, r=r, slot=slot: e.tensor_sub(
                    out=u_g[:, slot * 8 * TT + c * TT: slot * 8 * TT + (c + 1) * TT],
                    in0=u_g[:, slot * 8 * TT + c * TT: slot * 8 * TT + (c + 1) * TT],
                    in1=mm[:, :, r]), sem="VE")
            return ("VE", v)

        vgath = {}
        vgath[0] = gather_xhat(0, 0, ebar + [vlnall])
        for e_i in range(8):
            r0 = (3 * e_i) % 8
            r1 = (3 * e_i + 3) % 8
            s0 = e_i % 2
            s1_ = (e_i + 1) % 2
            vgath[1] = gather_xhat(
                r1, s1_,
                [vgath[0]] if e_i == 0 else [vgath[1], la.get("ug_used")])
            vb1 = S.dma("gpsimd", "sg", b1col[:], bexpc_d[:, e_i * 8:(e_i + 1) * 8],
                        waits=[la["b1_used"]] if la.get("b1_used") else [])
            vb2 = S.dma("gpsimd", "sg", b2row[:],
                        bexp_d[:, 8192 + e_i * 1024: 8192 + (e_i + 1) * 1024],
                        waits=[])
            vge = {}

            def emit_w1_gran(oc, w, e_i=e_i):
                vge[oc] = S.dma("gpsimd", "sg",
                                w1g[:, (oc % 2) * 1024:(oc % 2 + 1) * 1024]
                                .rearrange("p (k m) -> p k m", k=8),
                                w1_d[e_i, :, oc], waits=w)
            for oc in range(2):
                w = [la[f"w1mm{oc % 2}"]] if la.get(f"w1mm{oc % 2}") else []
                emit_w1_gran(oc, w)
            vhe = None
            for oc in range(8):
                bk = 2 + oc % 2
                for ri, (rr_, sl) in enumerate(((r0, s0), (r1, s1_))):
                    w = [("sg", vge[oc])]
                    if oc == 0 and ri == 0:
                        w += [vgath[0], vgath[1], ("sg", vb1)]
                        if e_i == 0:
                            w += ebar
                    if ri == 0 and la.get(f"exa{bk}"):
                        w.append(la[f"exa{bk}"])
                    if ri == 1:
                        w.append(("AE", vhe))
                    for k in range(NCH):
                        S.step("tensor", lambda e, oc=oc, k=k, bk=bk, sl=sl: e.matmul(
                            pb[bk][:], w1g[:, (oc % 2) * 1024 + k * 128:
                                           (oc % 2) * 1024 + (k + 1) * 128],
                            u_g[:, sl * 8 * TT + k * TT: sl * 8 * TT + (k + 1) * TT],
                            start=(k == 0), stop=(k == NCH - 1)),
                            waits=w if k == 0 else (), sem="TE")
                    vt = S.val("TE")
                    vhe = S.step("scalar", lambda e, oc=oc, bk=bk, ri=ri: e.activation(
                        out=h_e[:, oc * 1024 + ri * TT: oc * 1024 + (ri + 1) * TT],
                        in_=pb[bk][:], func=AF.Gelu, bias=b1col[:, oc:oc + 1]),
                        waits=[("TE", vt)], sem="AE")
                    la[f"exa{bk}"] = ("AE", vhe)
                la[f"w1mm{oc % 2}"] = ("TE", S.val("TE"))
                if oc + 2 < 8:
                    emit_w1_gran(oc + 2, [la[f"w1mm{oc % 2}"]])
            la["b1_used"] = ("AE", S.val("AE"))

            def emit_w2_gran(oc, w, e_i=e_i):
                vge[8 + oc] = S.dma("gpsimd", "sg",
                                    w2g[:, (oc % 2) * 1024:(oc % 2 + 1) * 1024]
                                    .rearrange("p (k m) -> p k m", k=8),
                                    w2_d[e_i, :, oc], waits=w)
            for oc in range(2):
                w = [la[f"w2mm{oc % 2}"]] if la.get(f"w2mm{oc % 2}") else []
                emit_w2_gran(oc, w)
            for oc in range(8):
                bk = 2 + oc % 2
                for ri, rr_ in enumerate((r0, r1)):
                    w = [("sg", vge[8 + oc])]
                    if oc == 0 and ri == 0:
                        w += [("AE", vhe), ("sg", vb2)]
                    if la.get(f"exo{bk}"):
                        w.append(la[f"exo{bk}"])
                    for k in range(NCH):
                        S.step("tensor", lambda e, oc=oc, k=k, bk=bk, ri=ri: e.matmul(
                            pb[bk][:], w2g[:, (oc % 2) * 1024 + k * 128:
                                           (oc % 2) * 1024 + (k + 1) * 128],
                            h_e[:, k * 1024 + ri * TT: k * 1024 + (ri + 1) * TT],
                            start=(k == 0), stop=False),
                            waits=w if k == 0 else (), sem="TE")
                    vt = S.step("tensor", lambda e, oc=oc, bk=bk: e.matmul(
                        pb[bk][:], b2row[0:1, oc * 128:(oc + 1) * 128],
                        ones_row[0:1, 0:TT], start=False, stop=True), sem="TE")
                    slot3 = (oc % 2) * 2 + ri
                    wv = [("TE", vt)]
                    if la.get(f"te3used{slot3}"):
                        wv.append(la[f"te3used{slot3}"])
                    vts = S.step("vector", lambda e, bk=bk, slot3=slot3: e.tensor_scalar(
                        out=te3[:, slot3 * TT:(slot3 + 1) * TT], in0=pb[bk][:],
                        scalar1=cst[:, 5:6], scalar2=None, op0=ALU.mult),
                        waits=wv, sem="VE")
                    la[f"exo{bk}"] = ("VE", vts)
                    write_first = (ri == 1 and rr_ != 0) or (ri == 0 and rr_ == 0)
                    vdd = S.dma("gpsimd", "sg",
                                delta_dram[:, oc, rr_, :],
                                te3[:, slot3 * TT:(slot3 + 1) * TT],
                                waits=[("VE", vts)])
                    S.q["gpsimd"][-1] = (
                        (lambda e, _o=delta_dram[:, oc, rr_, :],
                         _i=te3[:, slot3 * TT:(slot3 + 1) * TT],
                         _wf=write_first:
                         e.dma_start(out=_o, in_=_i,
                                     accum_op=(ALU.bypass if _wf else ALU.add))),
                        S.q["gpsimd"][-1][1], S.q["gpsimd"][-1][2])
                    la[f"te3used{slot3}"] = ("sg", vdd)
                la[f"w2mm{oc % 2}"] = ("TE", S.val("TE"))
                if oc + 2 < 8:
                    emit_w2_gran(oc + 2, [la[f"w2mm{oc % 2}"]])
            la["ug_used"] = ("TE", S.val("TE"))
            vgath[0] = vgath[1]
        v_exp_done = ("sg", S.val("sg"))

        pa = PA()
        sw1 = pa.take(16 * 8 * 128)
        sw3 = pa.take(16 * 8 * 128)
        sw2g = pa.take(2 * 16 * 128)
        h1 = pa.take(16 * TT)
        s1 = pa.take(2 * TT)
        t3 = pa.take(TT)
        sbar = [(k, S.val(k)) for k in ("TE", "VE", "AE")]
        v1 = S.dma("gpsimd", "sg", sw1, sw1_d.rearrange("p a k m -> p (a k m)"),
                   waits=sbar)
        v1 = S.dma("gpsimd", "sg", sw3, sw3_d.rearrange("p a k m -> p (a k m)"))
        la["xhat_consumed"] = ("TE", S.val("TE"))
        for i in range(K_TILES):
            t0 = i * TT
            vap = ln_apply(
                i, lambda i, c: x_res[:, c * T + i * TT: c * T + i * TT + TT],
                ln_ready, extra_waits=[la.get("xhat_consumed"), ("sg", v1)])
            vh1 = None
            for oc in range(16):
                bka = 2 if oc % 2 == 0 else 4
                bkb = 3 if oc % 2 == 0 else 5
                w = [("VE", vap)] if oc == 0 else []
                if i == 0 and oc < 2:
                    w += sbar
                if la.get(f"ffa{bka}"):
                    w.append(la[f"ffa{bka}"])
                for k in range(NCH):
                    S.step("tensor", lambda e, oc=oc, k=k, bka=bka: e.matmul(
                        pb[bka][:],
                        sw1.rearrange("p (n m) -> p n m", m=128)[:, oc * 8 + k, :],
                        xhat[:, k * TT:(k + 1) * TT],
                        start=(k == 0), stop=(k == NCH - 1)),
                        waits=w if k == 0 else (), sem="TE")
                wv1 = [("TE", S.val("TE"))]
                if la.get(f"s1used{oc % 2}"):
                    wv1.append(la[f"s1used{oc % 2}"])
                va1 = S.step("scalar", lambda e, oc=oc, bka=bka: e.activation(
                    out=s1[:, (oc % 2) * TT:(oc % 2 + 1) * TT], in_=pb[bka][:],
                    func=AF.Silu, bias=bcols[:, 26 + oc:27 + oc]),
                    waits=wv1, sem="AE")
                w2 = [la[f"ffb{bkb}"]] if la.get(f"ffb{bkb}") else []
                if i == 0 and oc < 2:
                    w2 += sbar
                vt3 = None
                for k in range(NCH):
                    vt3 = S.step("tensor", lambda e, oc=oc, k=k, bkb=bkb: e.matmul(
                        pb[bkb][:],
                        sw3.rearrange("p (n m) -> p n m", m=128)[:, oc * 8 + k, :],
                        xhat[:, k * TT:(k + 1) * TT],
                        start=(k == 0), stop=(k == NCH - 1)),
                        waits=w2 if k == 0 else (), sem="TE")
                vts = S.step("vector", lambda e, oc=oc, bkb=bkb: e.tensor_scalar(
                    out=t3[:, 0:TT], in0=pb[bkb][:], scalar1=bcols[:, 42 + oc:43 + oc],
                    scalar2=None, op0=ALU.add), waits=[("TE", vt3)], sem="VE")
                la[f"ffb{bkb}"] = ("VE", vts)
                vh1 = S.step("vector", lambda e, oc=oc: e.tensor_mul(
                    out=h1[:, oc * TT:(oc + 1) * TT],
                    in0=s1[:, (oc % 2) * TT:(oc % 2 + 1) * TT], in1=t3[:, 0:TT]),
                    waits=[("AE", va1)], sem="VE")
                la[f"s1used{oc % 2}"] = ("VE", vh1)
                la[f"ffa{bka}"] = ("VE", vh1)
            vg2 = {}

            def emit_sw2_gran(oc, w):
                vg2[oc] = S.dma("gpsimd", "sg",
                                sw2g[:, (oc % 2) * 2048:(oc % 2 + 1) * 2048]
                                .rearrange("p (k m) -> p k m", k=16),
                                sw2_d[:, oc], waits=w)
            for oc in range(2):
                w = [la[f"sw2mm{oc % 2}"]] if la.get(f"sw2mm{oc % 2}") else []
                emit_sw2_gran(oc, w)
            for oc in range(8):
                bk = 2 + oc % 2
                w = [("VE", vh1), ("sg", vg2[oc])]
                if la.get(f"ffo{bk}"):
                    w.append(la[f"ffo{bk}"])
                for k in range(16):
                    S.step("tensor", lambda e, oc=oc, k=k, bk=bk: e.matmul(
                        pb[bk][:], sw2g[:, (oc % 2) * 2048 + k * 128:
                                        (oc % 2) * 2048 + (k + 1) * 128],
                        h1[:, k * TT:(k + 1) * TT],
                        start=(k == 0), stop=False),
                        waits=w if k == 0 else (), sem="TE")
                vt = S.step("tensor", lambda e, oc=oc, bk=bk: e.matmul(
                    pb[bk][:], bias_row[0:1, 1024 + oc * 128:1024 + (oc + 1) * 128],
                    ones_row[0:1, 0:TT], start=False, stop=True), sem="TE")
                la[f"sw2mm{oc % 2}"] = ("TE", vt)
                if oc + 2 < 8:
                    emit_sw2_gran(oc + 2, [("TE", vt)])
                vts = S.step("vector", lambda e, oc=oc, bk=bk: e.tensor_scalar(
                    out=t3[:, 0:TT], in0=pb[bk][:], scalar1=cst[:, 4:5],
                    scalar2=None, op0=ALU.mult), waits=[("TE", vt)], sem="VE")
                vr = S.step("vector", lambda e, oc=oc, t0=t0: e.tensor_add(
                    out=x_res[:, oc * T + t0: oc * T + t0 + TT],
                    in0=x_res[:, oc * T + t0: oc * T + t0 + TT],
                    in1=t3[:, 0:TT]), sem="VE")
                la[f"ffo{bk}"] = ("VE", vr)
            la["xhat_consumed"] = ("TE", S.val("TE"))
        dtmp = h1
        vlast = None
        for i in range(K_TILES):
            vdl = S.dma("sync", "sq",
                        dtmp[:, 0:4096].rearrange("p (c e o) -> p c e o",
                                                  c=8, e=8),
                        delta_dram[:, :, :, i * 64:(i + 1) * 64],
                        waits=[v_exp_done, ("VE", S.val("VE")),
                               la.get("dtmp_used")])
            for c in range(NCH):
                vlast = S.step("vector", lambda e, c=c, i=i: e.tensor_add(
                    out=x_res[:, c * T + i * TT: c * T + i * TT + TT]
                    .rearrange("p (o e8) -> p o e8", e8=8),
                    in0=x_res[:, c * T + i * TT: c * T + i * TT + TT]
                    .rearrange("p (o e8) -> p o e8", e8=8),
                    in1=dtmp[:, c * 512:(c + 1) * 512]
                    .rearrange("p (e8 o) -> p o e8", o=64)),
                    waits=[("sq", vdl)] if c == 0 else (), sem="VE")
            la["dtmp_used"] = ("VE", vlast)
        return ("VE", vlast)

    def phase_attn_b():
        bar = [(k, S.val(k)) for k in ("TE", "VE", "AE", "sq", "sg")]
        pa = PA()
        win_sb = pa.take(16 * 8 * 128)
        wob_sb = pa.take(8 * 8 * 128)
        xpw1_sb = pa.take(8 * 128)
        xpw2_sb = pa.take(8 * 128)
        dtw_sb = pa.take(8 * 128, rows=64)
        xc_pre = pa.take(8 * TT)
        sz = pa.take(8 * TT)
        xc = pa.take(8 * TT)
        yb = pa.take(8 * TT)
        og = xc_pre
        xdbl = pa.take(TT, rows=64)
        f1 = pa.take(2 * TT).bitcast(F32)
        f2 = pa.take(2 * TT).bitcast(F32)
        f3 = pa.take(2 * TT).bitcast(F32)
        f4 = pa.take(2 * TT).bitcast(F32)
        ry = pa.take(TT)
        mury = pa.take(TT)
        brow2 = pa.take(TT, rows=1)
        crow2 = pa.take(TT, rows=1)
        acol = pa.take(16).bitcast(F32)
        carry = pa.take(16).bitcast(F32)
        tail = pa.take(16).bitcast(F32)

        v1 = S.dma("gpsimd", "sg", win_sb, win_d.rearrange("p a k m -> p (a k m)"),
                   waits=bar)
        S.dma("gpsimd", "sg", wob_sb, wob_d.rearrange("p a k m -> p (a k m)"))
        S.dma("gpsimd", "sg", xpw1_sb, xpw1_d.rearrange("p a m -> p (a m)"))
        S.dma("gpsimd", "sg", xpw2_sb, xpw2_d.rearrange("p a m -> p (a m)"))
        v1 = S.dma("gpsimd", "sg", dtw_sb, dtw_d.rearrange("p a m -> p (a m)"))
        ln_ready = ln_prepass(
            lambda i, c: x_res[:, c * T + i * TT: c * T + i * TT + TT],
            first_waits=bar)
        va0 = S.step("scalar", lambda e: e.activation(
            out=acol[:], in_=perpart[:, 32:40], func=AF.Exp), waits=bar, sem="AE")
        vA = S.step("vector", lambda e: e.tensor_scalar_mul(
            out=acol[:], in0=acol[:], scalar1=-1.0), waits=[("AE", va0)], sem="VE")
        S.step("vector", lambda e: e.memset(carry[:], 0.0), sem="VE")
        vA = S.step("vector", lambda e: e.memset(tail[:], 0.0), sem="VE")
        la["xhat_consumed"] = ("TE", S.val("TE"))

        for i in range(K_TILES):
            t0 = i * TT
            vap = ln_apply(
                i, lambda i, c: x_res[:, c * T + i * TT: c * T + i * TT + TT],
                ln_ready, extra_waits=[la.get("xhat_consumed"), ("sg", v1)])
            for oc in range(16):
                bk = 2 + oc % 2
                w = [("VE", vap)] if oc == 0 else []
                if i == 0 and oc < 2:
                    w += bar
                if la.get(f"bwin{bk}"):
                    w.append(la[f"bwin{bk}"])
                for k in range(NCH):
                    S.step("tensor", lambda e, oc=oc, k=k, bk=bk: e.matmul(
                        pb[bk][:],
                        win_sb.rearrange("p (n m) -> p n m", m=128)[:, oc * 8 + k, :],
                        xhat[:, k * TT:(k + 1) * TT],
                        start=(k == 0), stop=(k == NCH - 1)),
                        waits=w if k == 0 else (), sem="TE")
                vt = S.val("TE")
                if oc < 8:
                    va = S.step("scalar", lambda e, oc=oc, bk=bk: e.activation(
                        out=xc_pre[:, oc * TT:(oc + 1) * TT], in_=pb[bk][:],
                        func=AF.Copy), waits=[("TE", vt)], sem="AE")
                else:
                    va = S.step("scalar", lambda e, oc=oc, bk=bk: e.activation(
                        out=sz[:, (oc - 8) * TT:(oc - 7) * TT], in_=pb[bk][:],
                        func=AF.Silu), waits=[("TE", vt)], sem="AE")
                la[f"bwin{bk}"] = ("AE", va)
            v_winev = S.val("AE")
            vxc = None
            for g in range(NCH):
                w = [("AE", v_winev)]
                S.step("vector", lambda e, g=g: e.tensor_scalar(
                    out=f1[:, 1:TT], in0=xc_pre[:, g * TT:g * TT + TT - 1],
                    scalar1=perpart[:, g:g + 1], scalar2=None, op0=ALU.mult),
                    waits=w, sem="VE")
                S.step("vector", lambda e, g=g: e.tensor_scalar(
                    out=f1[:, 0:1], in0=tail[:, g:g + 1],
                    scalar1=perpart[:, g:g + 1], scalar2=None, op0=ALU.mult),
                    sem="VE")
                S.step("vector", lambda e, g=g: e.tensor_copy(
                    out=tail[:, g:g + 1],
                    in_=xc_pre[:, g * TT + TT - 1:g * TT + TT]), sem="VE")
                vcv = S.step("vector", lambda e, g=g: e.scalar_tensor_tensor(
                    out=f1[:, 0:TT], in0=xc_pre[:, g * TT:(g + 1) * TT],
                    scalar=perpart[:, 8 + g:9 + g], in1=f1[:, 0:TT],
                    op0=ALU.mult, op1=ALU.add), sem="VE")
                vxc = S.step("scalar", lambda e, g=g: e.activation(
                    out=xc[:, g * TT:(g + 1) * TT], in_=f1[:, 0:TT],
                    func=AF.Silu, bias=perpart[:, 16 + g:17 + g]),
                    waits=[("VE", vcv)], sem="AE")
            v_xcdone = vxc
            w = [("AE", v_xcdone)]
            if la.get("bxp"):
                w.append(la["bxp"])
            for k in range(NCH):
                S.step("tensor", lambda e, k=k: e.matmul(
                    pb[4][:],
                    xpw1_sb.rearrange("p (a m) -> p a m", m=128)[:, k, :],
                    xc[:, k * TT:(k + 1) * TT],
                    start=(k == 0), stop=(k == NCH - 1)),
                    waits=w if k == 0 else (), sem="TE")
            for k in range(NCH):
                vt2 = S.step("tensor", lambda e, k=k: e.matmul(
                    pb[5][:],
                    xpw2_sb.rearrange("p (a m) -> p a m", m=128)[:, k, :],
                    xc[:, k * TT:(k + 1) * TT],
                    start=(k == 0), stop=(k == NCH - 1)), sem="TE")
            vxd = S.step("scalar", lambda e: e.activation(
                out=xdbl[:, 0:TT], in_=pb[4][0:64, :], func=AF.Copy),
                waits=[("TE", vt2)], sem="AE")
            S.step("scalar", lambda e: e.activation(
                out=brow2[:, 0:TT], in_=pb[5][0:1, :], func=AF.Copy), sem="AE")
            vcr = S.step("scalar", lambda e: e.activation(
                out=crow2[:, 0:TT], in_=pb[5][64:65, :], func=AF.Copy), sem="AE")
            la["bxp"] = ("AE", vcr)
            vbb = S.step("tensor", lambda e: e.matmul(
                pb[4][:], ones_col[0:1, :], brow2[0:1, 0:TT],
                start=True, stop=True), waits=[("AE", vcr)], sem="TE")
            vcb = S.step("tensor", lambda e: e.matmul(
                pb[5][:], ones_col[0:1, :], crow2[0:1, 0:TT],
                start=True, stop=True), sem="TE")
            vyb = None
            for g in range(NCH):
                bk = 2 + g % 2
                w = [("AE", vxd)]
                if la.get(f"bdt{bk}"):
                    w.append(la[f"bdt{bk}"])
                vt = S.step("tensor", lambda e, g=g, bk=bk: e.matmul(
                    pb[bk][:], dtw_sb[:, g * 128:(g + 1) * 128], xdbl[:, 0:TT],
                    start=True, stop=True), waits=w, sem="TE")
                va = S.step("scalar", lambda e, g=g, bk=bk: e.activation(
                    out=f1[:, 0:TT], in_=pb[bk][:], func=AF.Abs,
                    bias=perpart[:, 24 + g:25 + g]), waits=[("TE", vt)], sem="AE")
                S.step("scalar", lambda e: e.activation(
                    out=f1[:, 0:TT], in_=f1[:, 0:TT], func=AF.Exp,
                    scale=cst[:, 7:8]), sem="AE")
                S.step("scalar", lambda e: e.activation(
                    out=f1[:, 0:TT], in_=f1[:, 0:TT], func=AF.Ln,
                    bias=cst[:, 6:7]), sem="AE")
                vsp = S.step("scalar", lambda e, g=g, bk=bk: e.activation(
                    out=f2[:, 0:TT], in_=pb[bk][:], func=AF.Relu,
                    bias=perpart[:, 24 + g:25 + g]), sem="AE")
                la[f"bdt{bk}"] = ("AE", vsp)
                vdl = S.step("vector", lambda e: e.tensor_add(
                    out=f2[:, 0:TT], in0=f2[:, 0:TT], in1=f1[:, 0:TT]),
                    waits=[("AE", vsp)], sem="VE")
                vaa = S.step("scalar", lambda e, g=g: e.activation(
                    out=f1[:, 0:TT], in_=f2[:, 0:TT], func=AF.Exp,
                    scale=acol[:, g:g + 1]), waits=[("VE", vdl)], sem="AE")
                S.step("vector", lambda e, g=g: e.tensor_mul(
                    out=f2[:, 0:TT], in0=f2[:, 0:TT],
                    in1=xc[:, g * TT:(g + 1) * TT]), sem="VE")
                vb2 = S.step("vector", lambda e: e.tensor_mul(
                    out=f2[:, 0:TT], in0=f2[:, 0:TT], in1=pb[4][:]),
                    waits=[("TE", vbb)], sem="VE")
                vsc = S.step("vector", lambda e, g=g, i=i: e.tensor_tensor_scan(
                    out=f3[:, 0:TT], data0=f1[:, 0:TT], data1=f2[:, 0:TT],
                    initial=(0.0 if i == 0 else carry[:, g:g + 1]),
                    op0=ALU.mult, op1=ALU.add), waits=[("AE", vaa)], sem="VE")
                S.step("vector", lambda e, g=g: e.tensor_copy(
                    out=carry[:, g:g + 1], in_=f3[:, TT - 1:TT]), sem="VE")
                S.step("vector", lambda e: e.tensor_mul(
                    out=f3[:, 0:TT], in0=f3[:, 0:TT], in1=pb[5][:]),
                    waits=[("TE", vcb)], sem="VE")
                vyb = S.step("vector", lambda e, g=g: e.scalar_tensor_tensor(
                    out=yb[:, g * TT:(g + 1) * TT],
                    in0=xc[:, g * TT:(g + 1) * TT],
                    scalar=perpart[:, 40 + g:41 + g], in1=f3[:, 0:TT],
                    op0=ALU.mult, op1=ALU.add), sem="VE")
            vt1 = None
            for c in range(NCH):
                w = [("VE", vyb)] if c == 0 else []
                if i == 0 and c == 0:
                    w.append(la["lnp_done"]) if la.get("lnp_done") else None
                vt1 = S.step("tensor", lambda e, c=c: e.matmul(
                    pb[0][:, 0:TT], ones_col[:], yb[:, c * TT:(c + 1) * TT],
                    start=(c == 0), stop=(c == NCH - 1)), waits=w, sem="TE")
            va = None
            for c in range(NCH):
                wv = [("VE", vyb)] if c == 0 else []
                va = S.step("scalar", lambda e, c=c: e.activation(
                    out=xsq[:, (c % 2) * TT:(c % 2 + 1) * TT],
                    in_=yb[:, c * TT:(c + 1) * TT], func=AF.Square),
                    waits=wv, sem="AE")
                vt1 = S.step("tensor", lambda e, c=c: e.matmul(
                    pb[1][:, 0:TT], ones_col[:],
                    xsq[:, (c % 2) * TT:(c % 2 + 1) * TT],
                    start=(c == 0), stop=(c == NCH - 1)),
                    waits=[("AE", va)], sem="TE")
            S.step("vector", lambda e: e.tensor_scalar(
                out=f1[:, 0:TT], in0=pb[0][:, 0:TT], scalar1=cst[:, 3:4],
                scalar2=None, op0=ALU.mult), waits=[("TE", vt1)], sem="VE")
            S.step("vector", lambda e: e.tensor_scalar(
                out=f2[:, 0:TT], in0=pb[1][:, 0:TT], scalar1=cst[:, 3:4],
                scalar2=None, op0=ALU.mult), sem="VE")
            S.step("vector", lambda e: e.tensor_mul(
                out=f3[:, 0:TT], in0=f1[:, 0:TT], in1=f1[:, 0:TT]), sem="VE")
            vv = S.step("vector", lambda e: e.tensor_sub(
                out=f2[:, 0:TT], in0=f2[:, 0:TT], in1=f3[:, 0:TT]), sem="VE")
            S.step("scalar", lambda e: e.activation(
                out=f2[:, 0:TT], in_=f2[:, 0:TT], func=AF.Ln, bias=cst[:, 0:1]),
                waits=[("VE", vv)], sem="AE")
            vry = S.step("scalar", lambda e: e.activation(
                out=ry[:, 0:TT], in_=f2[:, 0:TT], func=AF.Exp,
                scale=cst[:, 1:2]), sem="AE")
            vmy = S.step("vector", lambda e: e.tensor_mul(
                out=mury[:, 0:TT], in0=f1[:, 0:TT], in1=ry[:, 0:TT]),
                waits=[("AE", vry)], sem="VE")
            vog = None
            for g in range(NCH):
                S.step("vector", lambda e, g=g: e.tensor_mul(
                    out=og[:, g * TT:(g + 1) * TT],
                    in0=yb[:, g * TT:(g + 1) * TT], in1=ry[:, 0:TT]),
                    waits=[("VE", vmy)] if g == 0 else (), sem="VE")
                S.step("vector", lambda e, g=g: e.tensor_sub(
                    out=og[:, g * TT:(g + 1) * TT],
                    in0=og[:, g * TT:(g + 1) * TT], in1=mury[:, 0:TT]), sem="VE")
                S.step("vector", lambda e, g=g: e.tensor_scalar(
                    out=og[:, g * TT:(g + 1) * TT],
                    in0=og[:, g * TT:(g + 1) * TT],
                    scalar1=perpart[:, 48 + g:49 + g],
                    scalar2=perpart[:, 56 + g:57 + g],
                    op0=ALU.mult, op1=ALU.add), sem="VE")
                vog = S.step("vector", lambda e, g=g: e.tensor_mul(
                    out=og[:, g * TT:(g + 1) * TT],
                    in0=og[:, g * TT:(g + 1) * TT],
                    in1=sz[:, g * TT:(g + 1) * TT]), sem="VE")
            for oc in range(NCH):
                bk = 2 + oc % 2
                w = [("VE", vog)] if oc == 0 else []
                if la.get(f"bwo{bk}"):
                    w.append(la[f"bwo{bk}"])
                vt = None
                for k in range(NCH):
                    vt = S.step("tensor", lambda e, oc=oc, k=k, bk=bk: e.matmul(
                        pb[bk][:],
                        wob_sb.rearrange("p (n m) -> p n m", m=128)[:, oc * 8 + k, :],
                        og[:, k * TT:(k + 1) * TT],
                        start=(k == 0), stop=(k == NCH - 1)),
                        waits=w if k == 0 else (), sem="TE")
                vr = S.step("vector", lambda e, oc=oc, bk=bk, t0=t0: e.tensor_add(
                    out=x_res[:, oc * T + t0: oc * T + t0 + TT],
                    in0=x_res[:, oc * T + t0: oc * T + t0 + TT],
                    in1=pb[bk][:]), waits=[("TE", vt)], sem="VE")
                la[f"bwo{bk}"] = ("VE", vr)
            la["xhat_consumed"] = ("TE", S.val("TE"))
        return ("VE", S.val("VE"))

    done = ("sg", v_xload)
    if RUN_ATTN_A:
        done = phase_attn_a()
    if RUN_FFN1:
        done = phase_ffn(1)
    if RUN_ATTN_B:
        done = phase_attn_b()
    if RUN_FFN2:
        done = phase_ffn(2)

    v_out = S.dma("gpsimd", "sg",
                  out_t.rearrange("(c p) t -> p c t", p=128),
                  x_res[:].rearrange("p (c t) -> p c t", c=NCH),
                  waits=[done])
    S.em("gpsimd", lambda e: None, waits=[("sg", v_out)])

    with contextlib.ExitStack() as stack:
        sem_handles = {}
        for s_ in ("sq", "sg", "TE", "VE", "AE", "GE"):
            sem_handles[s_] = stack.enter_context(nc.semaphore(s_))
        with nc.Block() as block:
            def make_body(ename):
                def body(e):
                    waited = {}
                    for fn, waits, inc in S.q[ename]:
                        for (sem, v) in waits:
                            if waited.get(sem, -1) < v:
                                e.wait_ge(sem_handles[sem], v)
                                waited[sem] = v
                        ins = fn(e)
                        if inc is not None:
                            if ins is None:
                                raise RuntimeError("then_inc on None")
                            if ename in ("vector", "scalar"):
                                ins = e.drain()
                            ins.then_inc(sem_handles[inc[0]], inc[1])
                return body
            for ename in ENGINES:
                getattr(block, ename)(make_body(ename))
    return nc


_NC_CACHE = None


def _get_nc():
    global _NC_CACHE
    if _NC_CACHE is None:
        _NC_CACHE = build_program()
    return _NC_CACHE


def _pack_lhsT(W, n_k, n_oc, col_order=None):
    Wc = W[:, col_order] if col_order is not None else W
    K, M = Wc.shape
    assert K == n_k * 128 and M == n_oc * 128
    return np.ascontiguousarray(
        Wc.reshape(n_k, 128, n_oc, 128).transpose(1, 2, 0, 3))


def make_inputs(i):
    f32 = np.float32
    d = {}
    cols = np.concatenate([
        np.concatenate([np.arange(c * 64, (c + 1) * 64),
                        np.arange((c + 8) * 64, (c + 9) * 64)])
        for c in range(8)])
    d["wq_h"] = _pack_lhsT(np.asarray(i["L0_attn_a_Wq"], f32), 8, 8, cols)
    d["wdn_h"] = _pack_lhsT(np.asarray(i["L0_attn_a_Wdown"], f32), 8, 2)
    d["wup_h"] = _pack_lhsT(np.asarray(i["L0_attn_a_Wup"], f32), 2, 16)
    d["wout_h"] = _pack_lhsT(np.asarray(i["L0_attn_a_Wout"], f32), 8, 8)
    bcol = np.zeros((128, 64), f32)
    bcol[:, 0:8] = np.asarray(i["L0_attn_a_bq"], f32)[cols].reshape(8, 128).T
    bcol[:, 8:10] = np.asarray(i["L0_attn_a_bdown"], f32).reshape(2, 128).T
    bcol[:, 10:26] = np.asarray(i["L0_attn_a_bup"], f32).reshape(16, 128).T
    bcol[:, 26:42] = np.asarray(i["L0_ffn_sb1"], f32).reshape(16, 128).T
    bcol[:, 42:58] = np.asarray(i["L0_ffn_sb3"], f32).reshape(16, 128).T
    d["bcol_hbm"] = bcol
    brow = np.zeros((1, 2048), f32)
    brow[0, 0:1024] = np.asarray(i["L0_attn_a_bout"], f32)
    brow[0, 1024:2048] = np.asarray(i["L0_ffn_sb2"], f32)
    d["brow_hbm"] = brow
    d["onesrow"] = np.ones((1, TT), f32)
    d["identity"] = np.eye(128, dtype=f32)
    iz = np.zeros((128, 128), f32)
    iz[64:128, 0:64] = np.eye(64, dtype=f32)
    d["identz_h"] = iz

    d["sw1_h"] = _pack_lhsT(np.asarray(i["L0_ffn_sw1"], f32), 8, 16)
    d["sw3_h"] = _pack_lhsT(np.asarray(i["L0_ffn_sw3"], f32), 8, 16)
    d["sw2_h"] = _pack_lhsT(np.asarray(i["L0_ffn_sw2"], f32), 16, 8)
    d["w1_h"] = np.stack([_pack_lhsT(np.asarray(i["L0_ffn_w1"], f32)[e], 8, 8)
                        for e in range(8)])
    d["w2_h"] = np.stack([_pack_lhsT(np.asarray(i["L0_ffn_w2"], f32)[e], 8, 8)
                        for e in range(8)])
    bffn = np.zeros((1, 8192), f32)
    bffn[0, 0:2048] = np.asarray(i["L0_ffn_sb1"], f32)
    bffn[0, 2048:4096] = np.asarray(i["L0_ffn_sb3"], f32)
    bffn[0, 4096:5120] = np.asarray(i["L0_ffn_sb2"], f32)
    d["bffn_hbm"] = bffn
    bexp = np.zeros((1, 16384), f32)
    bexp[0, 0:8192] = np.asarray(i["L0_ffn_b1"], f32).reshape(-1)
    bexp[0, 8192:16384] = np.asarray(i["L0_ffn_b2"], f32).reshape(-1)
    d["bexp_hbm"] = bexp
    bexpc = np.zeros((128, 64), f32)
    b1全 = np.asarray(i["L0_ffn_b1"], f32)
    for e in range(8):
        bexpc[:, e * 8:(e + 1) * 8] = b1全[e].reshape(8, 128).T
    d["bexpc_hbm"] = bexpc

    d["win_h"] = _pack_lhsT(np.ascontiguousarray(
        np.asarray(i["L0_attn_b_w_in"], f32).T), 8, 16)
    d["wob_h"] = _pack_lhsT(np.ascontiguousarray(
        np.asarray(i["L0_attn_b_w_out"], f32).T), 8, 8)
    xp = np.asarray(i["L0_attn_b_x_proj_weight"], f32)
    xpw1 = np.zeros((1024, 128), f32)
    xpw1[:, 0:64] = xp[0:64].T
    d["xpw1_h"] = np.ascontiguousarray(xpw1.reshape(8, 128, 128).transpose(1, 0, 2))
    xpw2 = np.zeros((1024, 128), f32)
    xpw2[:, 0] = xp[64]
    xpw2[:, 64] = xp[65]
    d["xpw2_h"] = np.ascontiguousarray(xpw2.reshape(8, 128, 128).transpose(1, 0, 2))
    dtw = np.asarray(i["L0_attn_b_dt_projs_weight"], f32)
    d["dtw_h"] = np.ascontiguousarray(dtw.T.reshape(64, 8, 128))
    pp = np.zeros((128, 64), f32)

    def col8(v):
        return np.asarray(v, f32).reshape(8, 128).T
    cw = np.asarray(i["L0_attn_b_conv_w"], f32)[:, 0, :]
    pp[:, 0:8] = col8(cw[:, 0])
    pp[:, 8:16] = col8(cw[:, 1])
    pp[:, 16:24] = col8(i["L0_attn_b_conv_b"])
    pp[:, 24:32] = col8(i["L0_attn_b_dt_projs_bias"])
    pp[:, 32:40] = col8(np.asarray(i["L0_attn_b_A_logs"], f32)[:, 0])
    pp[:, 40:48] = col8(i["L0_attn_b_Ds"])
    pp[:, 48:56] = col8(i["L0_attn_b_ln_w"])
    pp[:, 56:64] = col8(i["L0_attn_b_ln_b"])
    d["pp_hbm"] = pp
    return d


def kernel(**inputs):
    x = np.asarray(inputs["L0_attn_a_x"], np.float32)
    base = make_inputs(inputs)
    nc = _get_nc()
    in_maps = []
    for b in range(8):
        m = dict(base)
        m["x_t"] = np.ascontiguousarray(x[b].T)
        in_maps.append(m)
    res = run_bass_kernel_spmd(nc, in_maps[:N_CORES], core_ids=list(range(N_CORES)))
    out = np.stack([np.asarray(r["out_t"], np.float32).T for r in res.results])
    if N_CORES < 8:
        out = np.concatenate([out] + [out[-1:]] * (8 - N_CORES))
    return out
